# version 28
# speedup vs baseline: 1.1886x; 1.0320x over previous
import numpy as np
import concourse.bass as bass
import concourse.mybir as mybir
from concourse.bass_utils import run_bass_kernel_spmd

F32 = mybir.dt.float32
BF16 = mybir.dt.bfloat16
AF = mybir.ActivationFunctionType
ALU = mybir.AluOpType
AX = mybir.AxisListType
I32 = mybir.dt.int32

S, D, H, HD, WIN = 8192, 2048, 16, 128, 256
NCORE = 8
T = S // NCORE
HALO = WIN
TH = T + HALO
NCH = D // 128
CPC = T // WIN
NTQ = S // 128
LN2 = float(np.log(2.0))
ISQHD = float(np.float32(1.0) / np.float32(np.sqrt(np.float32(128.0))))

DBG = ()

W_Q0, W_K0, W_V0, W_O0, W_Q2, W_O1, W_F1 = range(7)
WSL = 7 * 256



ENGS = ("sync", "vector", "scalar", "gpsimd", "tensor")
PAGE = 1024


class KB:
    def __init__(self, nc):
        self.nc = nc
        self.q = {e: [] for e in ENGS}
        self.esem = {}
        self.cnt = {}
        for e in ("vector", "scalar", "gpsimd", "tensor"):
            self.esem[e] = nc.ctx.enter_context(nc.semaphore("SEM_" + e))
            self.cnt[e] = 0
        self.last_w = {}
        self.readers = {}
        self.waited = {e: {} for e in ENGS}
        self.drain_cov = {e: 0 for e in ENGS}
        self.nins = 0
        self._pidv = {}

    def dch(self, qeng, ch):
        key = ("DG:" if qeng == "gpsimd" else "DY:") + ch
        if key not in self.esem:
            self.esem[key] = self.nc.ctx.enter_context(self.nc.semaphore("SEM_" + key.replace(":", "_")))
            self.cnt[key] = 0
        return key

    def pid(self, e):
        key = str(e.engine)
        if key not in self._pidv:
            self._pidv[key] = e.partition_id()
        return self._pidv[key]

    PSUM_NAMES = ("pa0", "pa1", "pa2", "pa3", "pb0", "ptp", "ptq", "psm")

    def pages(self, ap_or_tuple):
        name, b0, b1 = ap_or_tuple
        if name in self.PSUM_NAMES:
            return [(name, 0)]
        return [(name, p) for p in range(b0 // PAGE, (max(b1 - 1, b0)) // PAGE + 1)]

    def _wait(self, eng, src, val):
        if src == eng:
            if src == "tensor":
                return
            if self.drain_cov[eng] < val:
                self.emit_raw(eng, lambda e: e.drain())
                self.drain_cov[eng] = self.cnt[eng]
            return
        if src.startswith("DY:") or src.startswith("DG:"):
            val = self.cnt[src]
        w = self.waited[eng]
        if w.get(src, 0) >= val:
            return
        w[src] = val
        sem = self.esem[src]
        self.emit_raw(eng, lambda e, s=sem, v=val: e.wait_ge(s, v))

    def _dep_waits(self, eng, reads, writes):
        deps = {}
        for r in reads:
            for p in self.pages(r):
                lw = self.last_w.get(p)
                if lw:
                    deps[lw[0]] = max(deps.get(lw[0], 0), lw[1])
        for wreg in writes:
            for p in self.pages(wreg):
                lw = self.last_w.get(p)
                if lw:
                    deps[lw[0]] = max(deps.get(lw[0], 0), lw[1])
                for src, cval in self.readers.get(p, {}).items():
                    deps[src] = max(deps.get(src, 0), cval)
        for src, val in deps.items():
            self._wait(eng, src, val)

    def _record(self, tag, cnt, reads, writes):
        for r in reads:
            for p in self.pages(r):
                self.readers.setdefault(p, {})[tag] = cnt
        for wreg in writes:
            for p in self.pages(wreg):
                self.last_w[p] = (tag, cnt)
                self.readers[p] = {}

    def emit_raw(self, eng, fn):
        self.q[eng].append(fn)
        self.nins += 1

    def op(self, eng, fn, reads=(), writes=(), inc=True):
        self._dep_waits(eng, reads, writes)
        if inc:
            self.cnt[eng] += 1
            sem, val = self.esem[eng], self.cnt[eng]
            self.emit_raw(eng, lambda e, f=fn, s=sem: f(e).then_inc(s, 1))
            self._record(eng, val, reads, writes)
        else:
            self.emit_raw(eng, fn)
            self._record(eng, self.cnt[eng] + 10 ** 9, reads, writes)

    def mmgroup(self, fns, reads=(), writes=()):
        self._dep_waits("tensor", reads, writes)
        for f in fns[:-1]:
            self.emit_raw("tensor", f)
        self.cnt["tensor"] += 1
        sem, val = self.esem["tensor"], self.cnt["tensor"]
        self.emit_raw("tensor", lambda e, f=fns[-1], s=sem: f(e).then_inc(s, 1))
        self._record("tensor", val, reads, writes)

    def dma(self, qeng, out, in_, reads=(), writes=(), transpose=False, ch="mis"):
        dtag = self.dch(qeng, ch)
        self._dep_waits(qeng, reads, writes)
        self.cnt[dtag] += 16
        sem, val = self.esem[dtag], self.cnt[dtag]

        def go(e, o=out, i=in_, s=sem, t=transpose):
            if callable(o):
                o = o(e)
            if callable(i):
                i = i(e)
            if t:
                e.dma_start_transpose(out=o, in_=i).then_inc(s, 16)
            else:
                e.dma_start(out=o, in_=i).then_inc(s, 16)
        self.emit_raw(qeng, go)
        self._record(dtag, val, reads, writes)

    def collective(self, kind, ins, outs, reads=(), writes=()):
        self._dep_waits("gpsimd", reads, writes)
        self.cnt["gpsimd"] += 1
        sem, val = self.esem["gpsimd"], self.cnt["gpsimd"]
        self.emit_raw("gpsimd", lambda e, s=sem: e.collective_compute(
            kind, ALU.bypass, replica_groups=[list(range(NCORE))],
            ins=[a.opt() for a in ins], outs=[a.opt() for a in outs]).then_inc(s, 1))
        self._record("gpsimd", val, reads, writes)

    def fence_all(self, eng):
        for src in list(self.cnt):
            if src != eng and self.cnt[src]:
                self._wait(eng, src, self.cnt[src])



def build_nc():
    nc = bass.Bass("TRN2", target_bir_lowering=False)
    g = {}

    def din(name, shape, dt=F32):
        g[name] = nc.dram_tensor(name, list(shape), dt, kind="ExternalInput").ap()

    def dint(name, shape, dt=BF16):
        kind = "ExternalOutput" if name in DBG else "Internal"
        g[name] = nc.dram_tensor(name, list(shape), dt, kind=kind).ap()

    din("xh", (TH, D))
    for nm in ("wq0s", "wk0s", "wv0s", "wo0s", "w2s", "wo1s", "wf1s",
               "wq1s", "wk1s", "wv1s"):
        din(nm, (256, D))
    din("b1", (1, D))
    din("sc0", (128, 1))
    din("sc1", (128, 1))
    din("rcosT", (128, TH))
    din("rsinT", (128, TH))
    din("mask0", (256, 1024), BF16)
    din("tril1", (128, 128), BF16)
    din("trilneg", (128, 128))
    din("identb", (128, 128), BF16)
    din("rotmb", (128, 128), BF16)
    din("ones_c", (128, 1), BF16)
    din("ones_r", (1, 128), BF16)
    g["out"] = nc.dram_tensor("out", [T, D], F32, kind="ExternalOutput").ap()

    dint("WAGI", (WSL, D))
    dint("WAGO", (NCORE * WSL, D))
    dint("WT1H", (768, D))
    dint("XN0H", (D, TH))
    dint("QT0", (D, T))
    dint("KT0", (D, TH))
    dint("VT0", (D, TH))
    dint("S0H", (T, D))
    dint("X1H", (T, D), F32)
    dint("S1H", (T, D), F32)
    dint("XN1TOK", (T, D))
    if "Q1D" in DBG:
        dint("Q1D", (D, T))
    dint("XGI", (D, T))
    dint("XGO", (NCORE * D, T))
    dint("AOI", (256, S))
    dint("AOO", (D, S))

    sb = {}

    def sbuf(name, shape, dt=F32):
        sb[name] = nc.ctx.enter_context(nc.sbuf_tensor(name, list(shape), dt))

    def psum(name, shape, dt=F32):
        sb[name] = nc.ctx.enter_context(nc.psum_tensor(name, list(shape), dt))

    sbuf("AR1", (128, 32768), BF16)
    sbuf("XNA", (128, 20480), BF16)
    sbuf("AR4", (128, 8192), F32)
    sbuf("AR6", (128, 8192), BF16)
    sbuf("ev0", (128, 512)); sbuf("ev1", (128, 512))
    sbuf("ev2", (128, 512)); sbuf("ev3", (128, 512))
    sbuf("xsc", (128, 2048), BF16)
    sbuf("maskb", (128, 2048), BF16)
    sbuf("trilng", (128, 128))
    sbuf("identb_s", (128, 128), BF16)
    sbuf("rotmb_s", (128, 128), BF16)
    sbuf("onesc", (128, 1), BF16)
    sbuf("onesr", (1, 128), BF16)
    sbuf("sc0_s", (128, 1)); sbuf("sc1_s", (128, 1)); sbuf("invss", (128, 1))
    sbuf("bpos", (1, 2048), BF16)
    sbuf("repb", (1, 2), BF16)
    sbuf("cst", (128, 8))
    sbuf("evi", (128, 512), I32)
    sbuf("evj", (128, 512), I32)
    sbuf("ew0", (128, 512)); sbuf("ew1", (128, 512))
    sbuf("ew2", (128, 512)); sbuf("ew3", (128, 512))
    sbuf("xscl", (128, 8))
    sbuf("ln", (128, 16))
    sbuf("cola", (128, 8)); sbuf("colb", (128, 8))
    sbuf("sums", (128, 40))
    sbuf("nbq", (128, NTQ))
    sbuf("invq", (128, 8))
    sbuf("xstr0", (128, 1280), BF16)
    sbuf("xstr1", (128, 1280), BF16)
    sbuf("rhs0", (128, 512), BF16)
    sbuf("rhs1", (128, 512), BF16)
    sbuf("wstr", (128, 6144), BF16)
    sbuf("ptb", (128, 1024), BF16)

    psum("pa0", (128, 512)); psum("pa1", (128, 512))
    psum("pa2", (128, 512)); psum("pa3", (128, 512))
    psum("pb0", (128, 512))
    psum("ptp", (128, 1024), BF16)
    psum("psm", (128, 256))
    psum("ptq", (128, 1024), BF16)

    nc.const_aps.register(F32, 0.5, sb["cst"][:, 0:1])
    K = KB(nc)
    program(nc, g, sb, K)

    with nc.Block() as block:
        @block.sync
        def _(e):
            for f in K.q["sync"]:
                f(e)

        @block.vector
        def _(e):
            for f in K.q["vector"]:
                f(e)

        @block.scalar
        def _(e):
            for f in K.q["scalar"]:
                f(e)

        @block.gpsimd
        def _(e):
            for f in K.q["gpsimd"]:
                f(e)

        @block.tensor
        def _(e):
            for f in K.q["tensor"]:
                f(e)
    return nc


def R(name, c0, c1, esize=2):
    return (name, c0 * esize, c1 * esize)


def program(nc, g, sb, K):
    st_consts(nc, g, sb, K)
    st_ln0(nc, g, sb, K)
    st_wprep(nc, g, sb, K)
    st_l0proj(nc, g, sb, K)
    st_l0attn(nc, g, sb, K)
    st_l0ffn_gate(nc, g, sb, K)
    st_l1qkv_attn(nc, g, sb, K)
    st_l1out(nc, g, sb, K)



def st_consts(nc, g, sb, K):
    for j, val in enumerate((0.5, 1e-8, -3.0 * LN2, 3.0 * LN2, 1e-30)):
        K.op("vector", lambda e, j=j, val=val: e.memset(sb["cst"][:, j:j + 1], val),
             writes=[R("cst", j, j + 1, 4)])
    K.dma("sync", sb["trilng"][:, :], g["trilneg"][:, :], writes=[("trilng", 0, 512)])
    for nm, dst in (("mask0", "maskb"), ("identb", "identb_s"),
                    ("rotmb", "rotmb_s"), ("ones_c", "onesc"), ("ones_r", "onesr"),
                    ("sc0", "sc0_s"), ("sc1", "sc1_s")):
        if nm == "mask0":
            K.dma("sync", sb[dst][0:128, 0:1024], g[nm][0:128, :], writes=[R(dst, 0, 1024)])
            K.dma("sync", sb[dst][0:128, 1024:2048], g[nm][128:256, :], writes=[R(dst, 1024, 2048)])
        else:
            K.dma("sync", sb[dst][:, :], g[nm][:, :], writes=[(dst, 0, 4096)])
    K.dma("sync", sb["AR4"][:, 4096:5376], g["rcosT"][:, :], writes=[R("AR4", 4096, 5376, 4)])
    K.dma("sync", sb["AR4"][:, 5376:6656], g["rsinT"][:, :], writes=[R("AR4", 5376, 6656, 4)])
    K.dma("sync", sb["ev0"][0:1, 0:512], g["b1"][0:1, 0:512], writes=[R("ev0", 0, 512, 4)])
    K.dma("sync", sb["ev1"][0:1, 0:512], g["b1"][0:1, 512:1024], writes=[R("ev1", 0, 512, 4)])
    K.dma("sync", sb["ev2"][0:1, 0:512], g["b1"][0:1, 1024:1536], writes=[R("ev2", 0, 512, 4)])
    K.dma("sync", sb["ev3"][0:1, 0:512], g["b1"][0:1, 1536:2048], writes=[R("ev3", 0, 512, 4)])
    for j, srn in enumerate(("ev0", "ev1", "ev2", "ev3")):
        K.op("vector", lambda e, j=j, srn=srn: e.tensor_copy(
            out=sb["bpos"][0:1, 512 * j:512 * (j + 1)], in_=sb[srn][0:1, 0:512]),
            reads=[R(srn, 0, 512, 4)], writes=[R("bpos", 512 * j, 512 * (j + 1))])
    K.op("vector", lambda e: e.tensor_tensor(out=sb["invss"][:, :], in0=sb["sc0_s"][:, :],
                                             in1=sb["sc1_s"][:, :], op=ALU.mult),
         reads=[("sc0_s", 0, 4), ("sc1_s", 0, 4)], writes=[("invss", 0, 4)])
    K.op("vector", lambda e: e.reciprocal(out=sb["invss"][:, :], in_=sb["invss"][:, :]),
         reads=[("invss", 0, 4)], writes=[("invss", 0, 4)])


def st_ln0(nc, g, sb, K):
    ar4f = sb["AR4"]
    ln = sb["ln"]
    LNR = R("ln", 0, 16, 4)
    for it in range(10):
        xt = ar4f[:, 0:2048] if it % 2 == 0 else ar4f[:, 2048:4096]
        xreg = R("AR4", 0, 2048, 4) if it % 2 == 0 else R("AR4", 2048, 4096, 4)
        K.dma("sync", xt, g["xh"][128 * it:128 * (it + 1), :], writes=[xreg], ch="xa" if it % 2 == 0 else "xb")
        K.op("scalar", lambda e, xt=xt: e.activation(out=sb["xsc"][:, :], in_=xt, func=AF.Copy,
                                                     accum_out=ln[:, 0:1]),
             reads=[xreg], writes=[R("xsc", 0, 2048), LNR])
        K.op("scalar", lambda e, xt=xt: e.activation(out=sb["xsc"][:, :], in_=xt, func=AF.Square,
                                                     accum_out=ln[:, 1:2]),
             reads=[xreg], writes=[R("xsc", 0, 2048), LNR])
        K.op("scalar", lambda e: e.mul(out=ln[:, 2:3], in_=ln[:, 0:1], mul=1.0 / 2048.0),
             reads=[LNR], writes=[LNR])
        K.op("vector", lambda e: e.tensor_tensor(out=ln[:, 3:4], in0=ln[:, 2:3], in1=ln[:, 2:3], op=ALU.mult),
             reads=[LNR], writes=[LNR])
        K.op("vector", lambda e: e.tensor_scalar(out=ln[:, 4:5], in0=ln[:, 3:4],
                                                 scalar1=-1.0, scalar2=1e-5, op0=ALU.mult, op1=ALU.add),
             reads=[LNR], writes=[LNR])
        K.op("scalar", lambda e: e.activation(out=ln[:, 5:6], in_=ln[:, 1:2], func=AF.Ln,
                                              scale=1.0 / 2048.0, bias=ln[:, 4:5]),
             reads=[LNR], writes=[LNR])
        K.op("scalar", lambda e: e.activation(out=ln[:, 6:7], in_=ln[:, 5:6], func=AF.Exp, scale=-0.5),
             reads=[LNR], writes=[LNR])
        K.op("vector", lambda e: e.tensor_tensor(out=ln[:, 7:8], in0=ln[:, 2:3], in1=ln[:, 6:7], op=ALU.mult),
             reads=[LNR], writes=[LNR])
        K.op("vector", lambda e: e.tensor_scalar(out=ln[:, 8:9], in0=ln[:, 7:8], scalar1=-1.0,
                                                 scalar2=None, op0=ALU.mult),
             reads=[LNR], writes=[LNR])
        K.op("vector", lambda e, xt=xt: e.tensor_scalar(out=sb["xsc"][:, :], in0=xt,
                                                        scalar1=ln[:, 6:7], scalar2=ln[:, 8:9],
                                                        op0=ALU.mult, op1=ALU.add),
             reads=[xreg, LNR], writes=[R("xsc", 0, 2048)])
        for kk in range(NCH):
            K.op("tensor", lambda e, kk=kk: e.transpose(out=sb["ptp"][:, 0:128],
                                                        in_=sb["xsc"][:, 128 * kk:128 * (kk + 1)],
                                                        identity=sb["identb_s"][:, :]),
                 reads=[R("xsc", 128 * kk, 128 * (kk + 1)), ("identb_s", 0, 256)],
                 writes=[R("ptp", 0, 128)])
            c0 = TH * kk + 128 * it
            K.op("vector", lambda e, c0=c0: e.tensor_copy(out=sb["XNA"][:, c0:c0 + 128],
                                                          in_=sb["ptp"][:, 0:128]),
                 reads=[R("ptp", 0, 128)], writes=[R("XNA", c0, c0 + 128)])
    for kk in range(NCH):
        K.dma("sync", g["XN0H"][128 * kk:128 * (kk + 1), :], sb["XNA"][:, TH * kk:TH * (kk + 1)],
              reads=[R("XNA", TH * kk, TH * (kk + 1))], ch="xnst")


def st_wprep(nc, g, sb, K):
    ar1 = sb["AR1"]
    wslb = lambda loc: ar1[:, 8192 + 2048 * loc: 8192 + 2048 * (loc + 1)]
    WSLB = lambda loc: R("AR1", 8192 + 2048 * loc, 8192 + 2048 * (loc + 1))
    ar1f = ar1.bitcast(F32)
    wsl32 = lambda loc: ar1f[:, 2048 * loc:2048 * (loc + 1)]
    WSL32 = lambda loc: R("AR1", 4096 * loc, 4096 * (loc + 1))

    for idx, nm in ((W_Q0, "wq0s"), (W_K0, "wk0s"), (W_V0, "wv0s"), (W_O0, "wo0s"), (W_O1, "wo1s")):
        for loc in range(2):
            K.dma("gpsimd", wslb(loc), g[nm][128 * loc:128 * (loc + 1), :], writes=[WSLB(loc)], ch="wp%d" % loc)
            K.dma("gpsimd", g["WAGI"][idx * 256 + 128 * loc: idx * 256 + 128 * (loc + 1), :],
                  wslb(loc), reads=[WSLB(loc)], writes=[("WAGI", 0, 1)], ch="wgi")
    for loc in range(2):
        K.dma("sync", wsl32(loc), g["w2s"][128 * loc:128 * (loc + 1), :], writes=[WSL32(loc)], ch="w32")
        fp8_emu(nc, sb, K, wsl32(loc), WSL32(loc), wslb(loc), WSLB(loc), "sc1_s")
        K.dma("gpsimd", g["WAGI"][W_Q2 * 256 + 128 * loc: W_Q2 * 256 + 128 * (loc + 1), :],
              wslb(loc), reads=[WSLB(loc)], writes=[("WAGI", 0, 1)], ch="wgi")
    for loc in range(2):
        K.dma("sync", wsl32(loc), g["wf1s"][128 * loc:128 * (loc + 1), :], writes=[WSL32(loc)], ch="w32")
        LNR = R("ln", 0, 16, 4)
        K.op("scalar", lambda e, loc=loc: e.activation(out=sb["xsc"][:, :], in_=wsl32(loc), func=AF.Abs,
                                                       accum_out=sb["ln"][:, 9:10]),
             reads=[WSL32(loc)], writes=[R("xsc", 0, 2048), LNR])
        K.op("vector", lambda e: e.tensor_scalar(out=sb["ln"][:, 10:11], in0=sb["ln"][:, 9:10],
                                                 scalar1=1.0 / 2048.0, scalar2=1e-8,
                                                 op0=ALU.mult, op1=ALU.add),
             reads=[LNR], writes=[LNR])
        K.op("vector", lambda e: e.reciprocal(out=sb["ln"][:, 11:12], in_=sb["ln"][:, 10:11]),
             reads=[LNR], writes=[LNR])
        for cb in range(4):
            wv = wsl32(loc)[:, 512 * cb:512 * (cb + 1)]
            E0, E2, EI = R("ev0", 0, 512, 4), R("ev2", 0, 512, 4), R("evi", 0, 512, 4)
            K.op("vector", lambda e, wv=wv: e.tensor_scalar(out=sb["ev0"][:, :], in0=wv,
                                                            scalar1=sb["ln"][:, 11:12], scalar2=None,
                                                            op0=ALU.mult),
                 reads=[WSL32(loc), LNR], writes=[E0])
            K.op("vector", lambda e: e.tensor_copy(out=sb["evi"][:, :], in_=sb["ev0"][:, :]),
                 reads=[E0], writes=[EI])
            K.op("vector", lambda e: e.tensor_scalar(out=sb["ev2"][:, :], in0=sb["evi"][:, :],
                                                     scalar1=1.0, scalar2=-1.0, op0=ALU.min, op1=ALU.max),
                 reads=[EI], writes=[E2])
            K.op("vector", lambda e, loc=loc, cb=cb: e.tensor_scalar(
                out=wslb(loc)[:, 512 * cb:512 * (cb + 1)], in0=sb["ev2"][:, :],
                scalar1=sb["ln"][:, 9:10], scalar2=1.0 / 2048.0, op0=ALU.mult, op1=ALU.mult),
                reads=[E2, LNR], writes=[WSLB(loc)])
        K.dma("gpsimd", g["WAGI"][W_F1 * 256 + 128 * loc: W_F1 * 256 + 128 * (loc + 1), :],
              wslb(loc), reads=[WSLB(loc)], writes=[("WAGI", 0, 1)], ch="wgi")

    K.collective("AllGather", [g["WAGI"]], [g["WAGO"]],
                 reads=[("WAGI", 0, 1)], writes=[("WAGO", 0, 1)])

    for wi, nm in enumerate(("wq1s", "wk1s", "wv1s")):
        for loc in range(2):
            K.dma("gpsimd", wslb(loc), g[nm][128 * loc:128 * (loc + 1), :], writes=[WSLB(loc)], ch="wp%d" % loc)
            for kk in range(NCH):
                K.op("tensor", lambda e, loc=loc, kk=kk: e.transpose(
                    out=sb["ptp"][:, 0:128], in_=wslb(loc)[:, 128 * kk:128 * (kk + 1)],
                    identity=sb["identb_s"][:, :]),
                    reads=[WSLB(loc), ("identb_s", 0, 256)], writes=[R("ptp", 0, 128)])
                K.op("vector", lambda e, kk=kk: e.tensor_copy(out=sb["wstr"][:, 128 * kk:128 * (kk + 1)],
                                                              in_=sb["ptp"][:, 0:128]),
                     reads=[R("ptp", 0, 128)], writes=[R("wstr", 128 * kk, 128 * (kk + 1))])
            r0 = 128 * (wi * 2 + loc)
            K.dma("sync", g["WT1H"][r0:r0 + 128, :], sb["wstr"][:, 0:2048],
                  reads=[R("wstr", 0, 2048)], writes=[("WT1H", 0, 1)], ch="wt1")


def fp8_emu(nc, sb, K, src32, SRCR, dst_bf, DSTR, scname):
    for cb in range(4):
        fp8_chunk(nc, sb, K, src32[:, 512 * cb:512 * (cb + 1)], SRCR,
                  dst_bf[:, 512 * cb:512 * (cb + 1)], DSTR, scname, par=cb)


def wago_rows(g, idx, kk):
    rank, loc = kk // 2, kk % 2
    r0 = rank * WSL + idx * 256 + loc * 128
    return g["WAGO"][r0:r0 + 128, :]



def repl_scalar(K, sb, src_ap, SRCR, dst_ap, DSTR):
    K.op("vector", lambda e: e.tensor_copy(out=sb["repb"][0:1, 0:1], in_=src_ap),
         reads=[SRCR], writes=[("repb", 0, 4)])
    K.mmgroup([lambda e: e.matmul(out=sb["pb0"][0:128, 0:1], lhsT=sb["onesr"][0:1, :],
                                  rhs=sb["repb"][0:1, 0:1], start=True, stop=True)],
              reads=[("repb", 0, 4), ("onesr", 0, 2)], writes=[("pb0", 0, 2048)])
    K.op("vector", lambda e: e.tensor_copy(out=dst_ap, in_=sb["pb0"][0:128, 0:1]),
         reads=[("pb0", 0, 2048)], writes=[DSTR])

def st_l0proj(nc, g, sb, K):
    ar1 = sb["AR1"]
    WF = lambda kk: R("AR1", 2048 * kk, 2048 * (kk + 1))
    XN = lambda kk: R("XNA", TH * kk, TH * (kk + 1))
    COSR = R("AR4", 4096, 5376, 4)
    SINR = R("AR4", 5376, 6656, 4)

    for wi, widx in enumerate((W_Q0, W_K0, W_V0)):
        for kk in range(NCH):
            K.dma("sync", ar1[:, 2048 * kk:2048 * (kk + 1)], wago_rows(g, widx, kk),
                  reads=[("WAGO", 0, 1)], writes=[WF(kk)], ch="wl")
        isq = wi == 0
        passes = ([[(HALO, 512), (HALO + 512, 512)]] if isq
                  else [[(0, 512), (512, 512)], [(1024, 256)]])
        outT = ("QT0", "KT0", "VT0")[wi]
        for tcols in passes:
         for dq in range(NCH):
            pacs = (["pa0", "pa1"] if dq % 2 == 0 else ["pa2", "pa3"])[:len(tcols)]
            for kk in range(NCH):
                fns = []
                for bi, (c0, w) in enumerate(tcols):
                    fns.append(lambda e, kk=kk, dq=dq, c0=c0, w=w, pac=pacs[bi],
                               s=(kk == 0), p=(kk == NCH - 1):
                               e.matmul(out=sb[pac][0:128, 0:w],
                                        lhsT=ar1[:, 2048 * kk + 128 * dq: 2048 * kk + 128 * (dq + 1)],
                                        rhs=sb["XNA"][:, TH * kk + c0: TH * kk + c0 + w],
                                        start=s, stop=p))
                K.mmgroup(fns, reads=[WF(kk), XN(kk)],
                          writes=[(p_, 0, 2048) for p_ in pacs])
            for bi, (c0, w) in enumerate(tcols):
                pac = pacs[bi]
                par = (pac, 0, 2048)
                alt = (dq * 3 + bi) % 2
                qtm = sb["rhs0"] if alt == 0 else sb["rhs1"]
                QTM = R("rhs0" if alt == 0 else "rhs1", 0, 512)
                K.op("vector", lambda e, pac=pac, w=w, qtm=qtm: e.tensor_copy(
                    out=qtm[:, 0:w], in_=sb[pac][0:128, 0:w]), reads=[par], writes=[QTM])
                xo = sb["xsc"][:, 512 * alt:512 * alt + 512]
                XO = R("xsc", 512 * alt, 512 * alt + 512)
                if wi == 2:
                    K.op("vector", lambda e, qtm=qtm, w=w, xo=xo: e.tensor_copy(
                        out=xo[:, 0:w] if False else xo[0:128, 0:w], in_=qtm[:, 0:w]),
                        reads=[QTM], writes=[XO])
                else:
                    pro = "pb0"
                    PRO = (pro, 0, 2048)
                    K.mmgroup([lambda e, qtm=qtm, w=w, pro=pro: e.matmul(
                        out=sb[pro][0:128, 0:w], lhsT=sb["rotmb_s"][:, :], rhs=qtm[:, 0:w],
                        start=True, stop=True)],
                        reads=[QTM, ("rotmb_s", 0, 256)], writes=[PRO])
                    eva = sb["ev0"] if alt == 0 else sb["ev1"]
                    EVA = R("ev0" if alt == 0 else "ev1", 0, 512, 4)
                    evb = sb["ev2"] if alt == 0 else sb["ev3"]
                    EVB = R("ev2" if alt == 0 else "ev3", 0, 512, 4)
                    K.op("gpsimd", lambda e, qtm=qtm, c0=c0, w=w, eva=eva: e.tensor_tensor(
                        out=eva[:, 0:w], in0=qtm[:, 0:w],
                        in1=sb["AR4"][:, 4096 + c0:4096 + c0 + w], op=ALU.mult),
                        reads=[QTM, COSR], writes=[EVA])
                    K.op("vector", lambda e, pro=pro, c0=c0, w=w, evb=evb: e.tensor_tensor(
                        out=evb[:, 0:w], in0=sb[pro][0:128, 0:w],
                        in1=sb["AR4"][:, 5376 + c0:5376 + c0 + w], op=ALU.mult),
                        reads=[PRO, SINR], writes=[EVB])
                    K.op("vector", lambda e, eva=eva, evb=evb, w=w, xo=xo: e.tensor_tensor(
                        out=xo[0:128, 0:w], in0=eva[:, 0:w], in1=evb[:, 0:w], op=ALU.add),
                        reads=[EVA, EVB], writes=[XO])
                tgt0 = c0 - HALO if isq else c0
                K.dma("sync", g[outT][128 * dq:128 * (dq + 1), tgt0:tgt0 + w], xo[0:128, 0:w],
                      reads=[XO], writes=[(outT, 0, 1)], ch="pst")


def st_l0attn(nc, g, sb, K):
    ar1 = sb["AR1"]
    WF = lambda kk: R("AR1", 2048 * kk, 2048 * (kk + 1))
    for kk in range(NCH):
        K.dma("sync", ar1[:, 2048 * kk:2048 * (kk + 1)], wago_rows(g, W_O0, kk),
              reads=[("WAGO", 0, 1)], writes=[WF(kk)], ch="wl")

    xna = sb["XNA"]
    VW = lambda j: R("XNA", 8192 + 2048 * j, 8192 + 2048 * (j + 1))
    ar4f = sb["AR4"]
    ar4b = ar4f.bitcast(BF16)
    EXPR = R("AR4", 0, 512, 4)
    AOT = lambda hh, ts_: R("AR4", (12288 + 256 * hh + 128 * ts_) * 2 // 4,
                            (12288 + 256 * hh + 128 * (ts_ + 1)) * 2 // 4, 4)

    for ci in range(CPC):
        cpar = ci % 2
        if cpar == 0:
            kwb, kwn, kw0 = xna, "XNA", 0
            qcb, qcn, qc0 = xna, "XNA", 16384
        else:
            kwb, kwn, kw0 = sb["AR6"], "AR6", 0
            qcb, qcn, qc0 = ar4b, "AR4", 2048
        KW = lambda kk, kwn=kwn, kw0=kw0: R(kwn, kw0 + 512 * kk, kw0 + 512 * (kk + 1))
        QC = R(qcn, qc0, qc0 + 4096)
        for kk in range(NCH):
            K.dma("sync", kwb[:, kw0 + 512 * kk:kw0 + 512 * (kk + 1)],
                  g["KT0"][128 * kk:128 * (kk + 1), WIN * ci:WIN * ci + 512],
                  reads=[("KT0", 0, 1)], writes=[KW(kk)], ch="kw%d" % cpar)
            K.dma("sync", qcb[:, qc0 + 256 * kk:qc0 + 256 * (kk + 1)],
                  g["QT0"][128 * kk:128 * (kk + 1), WIN * ci:WIN * (ci + 1)],
                  reads=[("QT0", 0, 1)], writes=[QC], ch="qc%d" % cpar)
        for j in range(4):
            K.dma("sync", xna[:, 8192 + 2048 * j:8192 + 2048 * (j + 1)],
                  g["VT0"][0:2048, WIN * ci + 128 * j: WIN * ci + 128 * (j + 1)],
                  reads=[("VT0", 0, 1)], writes=[VW(j)], transpose=True, ch="vw")
        for hh in range(H):
            sq0 = 512 * (hh % 2)
            K.op("scalar", lambda e, hh=hh, sq0=sq0: e.activation(
                out=sb["xsc"][:, sq0:sq0 + 512], in_=kwb[:, kw0 + 512 * hh:kw0 + 512 * (hh + 1)],
                func=AF.Square),
                reads=[KW(hh)], writes=[R("xsc", sq0, sq0 + 512)])
            K.mmgroup([lambda e, sq0=sq0: e.matmul(out=sb["pb0"][0:1, 0:512], lhsT=sb["onesc"][:, :],
                                                   rhs=sb["xsc"][:, sq0:sq0 + 512],
                                                   start=True, stop=True)],
                      reads=[R("xsc", sq0, sq0 + 512), ("onesc", 0, 2)],
                      writes=[("pb0", 0, 2048)])
            K.op("vector", lambda e: e.reduce_max(out=sb["sums"][0:1, 16:17], in_=sb["pb0"][0:1, 0:512],
                                                  axis=AX.X),
                 reads=[("pb0", 0, 2048)], writes=[R("sums", 16, 17, 4)])
            K.op("scalar", lambda e: e.activation(out=sb["sums"][0:1, 17:18], in_=sb["sums"][0:1, 16:17],
                                                  func=AF.Sqrt, scale=1.0 / 128.0),
                 reads=[R("sums", 16, 17, 4)], writes=[R("sums", 17, 18, 4)])
            K.op("scalar", lambda e: e.mul(out=sb["sums"][0:1, 18:19], in_=sb["sums"][0:1, 17:18], mul=-1.0),
                 reads=[R("sums", 17, 18, 4)], writes=[R("sums", 18, 19, 4)])
            repl_scalar(K, sb, sb["sums"][0:1, 18:19], R("sums", 18, 19, 4),
                        sb["colb"][:, 2:3], R("colb", 2, 3, 4))
            for ts_ in range(2):
                par = (2 * hh + ts_) % 2
                qsl = qcb[:, qc0 + 256 * hh + 128 * ts_: qc0 + 256 * hh + 128 * (ts_ + 1)]
                K.op("scalar", lambda e, qsl=qsl, par=par: e.activation(
                    out=sb["xsc"][:, 1024 + 256 * par:1152 + 256 * par], in_=qsl, func=AF.Square),
                    reads=[QC], writes=[R("xsc", 1024 + 256 * par, 1152 + 256 * par)])
                K.mmgroup([lambda e, par=par: e.matmul(out=sb["pb0"][0:128, 2 + par:3 + par],
                                                       lhsT=sb["xsc"][:, 1024 + 256 * par:1152 + 256 * par],
                                                       rhs=sb["onesc"][:, :], start=True, stop=True)],
                          reads=[R("xsc", 1024 + 256 * par, 1152 + 256 * par), ("onesc", 0, 2)],
                          writes=[("pb0", 8 + 4 * par, 12 + 4 * par)])
                K.op("scalar", lambda e, par=par: e.activation(out=sb["colb"][:, 4 + par:5 + par],
                                                               in_=sb["pb0"][0:128, 2 + par:3 + par],
                                                               func=AF.Sqrt),
                     reads=[("pb0", 8 + 4 * par, 12 + 4 * par)], writes=[R("colb", 4 + par, 5 + par, 4)])
                K.op("vector", lambda e, par=par: e.tensor_scalar(out=sb["colb"][:, 6 + par:7 + par],
                                                                  in0=sb["colb"][:, 4 + par:5 + par],
                                                                  scalar1=sb["colb"][:, 2:3], scalar2=None,
                                                                  op0=ALU.mult),
                     reads=[R("colb", 4 + par, 5 + par, 4), R("colb", 2, 3, 4)],
                     writes=[R("colb", 6 + par, 7 + par, 4)])
                psc = "pa2" if par == 0 else "pa3"
                K.mmgroup([lambda e, qsl=qsl, psc=psc, hh=hh, kwb=kwb, kw0=kw0: e.matmul(
                    out=sb[psc][0:128, 0:512], lhsT=qsl,
                    rhs=kwb[:, kw0 + 512 * hh:kw0 + 512 * (hh + 1)], start=True, stop=True)],
                    reads=[QC, KW(hh)], writes=[(psc, 0, 2048)])
                var = 0 if ci == 0 else 1
                K.op("vector", lambda e, psc=psc, ts_=ts_, var=var: e.tensor_tensor(
                    out=sb[psc][0:128, 0:512], in0=sb[psc][0:128, 0:512],
                    in1=sb["maskb"][:, 1024 * ts_ + 512 * var: 1024 * ts_ + 512 * (var + 1)], op=ALU.add),
                    reads=[(psc, 0, 2048), R("maskb", 1024 * ts_ + 512 * var, 1024 * ts_ + 512 * (var + 1))],
                    writes=[(psc, 0, 2048)])
                K.op("scalar", lambda e, psc=psc, par=par: e.activation(
                    out=ar4f[:, 512 * par:512 * (par + 1)], in_=sb[psc][0:128, 0:512],
                    func=AF.Exp, scale=ISQHD, bias=sb["colb"][:, 6 + par:7 + par],
                    accum_out=sb["sums"][:, 4 * par:4 * par + 1]),
                    reads=[(psc, 0, 2048), R("colb", 6 + par, 7 + par, 4)],
                    writes=[R("AR4", 512 * par, 512 * (par + 1), 4), R("sums", 4 * par, 4 * par + 1, 4)])
                K.op("vector", lambda e, par=par: e.reciprocal(out=sb["sums"][:, 4 * par + 1:4 * par + 2],
                                                               in_=sb["sums"][:, 4 * par:4 * par + 1]),
                     reads=[R("sums", 4 * par, 4 * par + 1, 4)],
                     writes=[R("sums", 4 * par + 1, 4 * par + 2, 4)])
                prb = sb["wstr"][:, 512 * par:512 * (par + 1)]
                PRB = R("wstr", 512 * par, 512 * (par + 1))
                K.op("vector", lambda e, prb=prb, par=par: e.tensor_scalar(
                    out=prb, in0=ar4f[:, 512 * par:512 * (par + 1)],
                    scalar1=sb["sums"][:, 4 * par + 1:4 * par + 2], scalar2=None, op0=ALU.mult),
                    reads=[R("AR4", 512 * par, 512 * (par + 1), 4),
                           R("sums", 4 * par + 1, 4 * par + 2, 4)], writes=[PRB])
                ptn = "ptp" if par == 0 else "ptq"
                for j in range(4):
                    K.op("tensor", lambda e, prb=prb, j=j, ptn=ptn: e.transpose(
                        out=sb[ptn][:, 128 * j:128 * (j + 1)],
                        in_=prb[:, 128 * j:128 * (j + 1)],
                        identity=sb["identb_s"][:, :]),
                        reads=[PRB, ("identb_s", 0, 256)],
                        writes=[(ptn, 0, 1)])
                K.op("vector", lambda e, par=par, ptn=ptn: e.tensor_copy(
                    out=sb["ptb"][:, 512 * par:512 * (par + 1)], in_=sb[ptn][:, 0:512]),
                     reads=[(ptn, 0, 1)],
                     writes=[R("ptb", 512 * par, 512 * (par + 1))])
                fns = []
                for j in range(4):
                    fns.append(lambda e, j=j, hh=hh, par=par: e.matmul(
                        out=sb["psm"][0:128, 128 * par:128 * (par + 1)],
                        lhsT=xna[:, 8192 + 2048 * j + 128 * hh: 8192 + 2048 * j + 128 * (hh + 1)],
                        rhs=sb["ptb"][:, 512 * par + 128 * j:512 * par + 128 * (j + 1)],
                        start=(j == 0), stop=(j == 3)))
                K.mmgroup(fns, reads=[R("ptb", 512 * par, 512 * (par + 1))] + [VW(j) for j in range(4)],
                          writes=[("psm", 512 * par, 512 * par + 512)])
                dst = ar4b[:, 12288 + 256 * hh + 128 * ts_: 12288 + 256 * hh + 128 * (ts_ + 1)]
                K.op("vector", lambda e, dst=dst, par=par: e.tensor_copy(
                    out=dst, in_=sb["psm"][0:128, 128 * par:128 * (par + 1)]),
                    reads=[("psm", 512 * par, 512 * par + 512)], writes=[AOT(hh, ts_)])
        for ts_ in range(2):
            for kk in range(NCH):
                fns = []
                for nb in range(4):
                    fns.append(lambda e, kk=kk, nb=nb, ts_=ts_, s=(kk == 0), p=(kk == NCH - 1):
                               e.matmul(out=sb["pa%d" % nb][0:128, 0:512],
                                        lhsT=ar4b[:, 12288 + 256 * kk + 128 * ts_: 12288 + 256 * kk + 128 * (ts_ + 1)],
                                        rhs=ar1[:, 2048 * kk + 512 * nb: 2048 * kk + 512 * (nb + 1)],
                                        start=s, stop=p))
                K.mmgroup(fns, reads=[AOT(kk, ts_), WF(kk)],
                          writes=[("pa%d" % nb, 0, 2048) for nb in range(4)])
            for nb in range(4):
                alt = nb % 2
                st = sb["rhs0"] if alt == 0 else sb["rhs1"]
                STR_ = R("rhs0" if alt == 0 else "rhs1", 0, 512)
                K.op("vector", lambda e, nb=nb, st=st: e.tensor_copy(out=st[:, 0:512],
                                                                     in_=sb["pa%d" % nb][0:128, 0:512]),
                     reads=[("pa%d" % nb, 0, 2048)], writes=[STR_])
                r0 = WIN * ci + 128 * ts_
                K.dma("sync", g["S0H"][r0:r0 + 128, 512 * nb:512 * (nb + 1)], st[:, 0:512],
                      reads=[STR_], writes=[("S0H", 0, 1)], ch="s0st")


def st_l0ffn_gate(nc, g, sb, K):
    ar1 = sb["AR1"]
    WF = lambda kk: R("AR1", 2048 * kk, 2048 * (kk + 1))
    for kk in range(NCH):
        K.dma("sync", ar1[:, 2048 * kk:2048 * (kk + 1)], wago_rows(g, W_Q2, kk),
              reads=[("WAGO", 0, 1)], writes=[WF(kk)], ch="wl")
    ar4b = sb["AR4"].bitcast(BF16)
    Q1 = lambda kk: R("AR4", 1024 * kk * 2 // 4, 1024 * (kk + 1) * 2 // 4, 4)

    for kk in range(NCH):
        xs = sb["xstr0"] if kk % 2 == 0 else sb["xstr1"]
        XSR = R("xstr0" if kk % 2 == 0 else "xstr1", 0, TH)
        K.dma("sync", xs[:, 0:T], g["XN0H"][128 * kk:128 * (kk + 1), HALO:TH],
              reads=[("XN0H", 0, 1)], writes=[XSR], ch="xs%d" % (kk % 2))
        for cb in range(2):
            fp8_chunk(nc, sb, K, xs[:, 512 * cb:512 * (cb + 1)], XSR,
                      ar4b[:, 1024 * kk + 512 * cb:1024 * kk + 512 * (cb + 1)], Q1(kk), "sc0_s",
                      par=2 * kk + cb)
        if "Q1D" in DBG:
            K.dma("sync", g["Q1D"][128 * kk:128 * (kk + 1), :],
                  ar4b[:, 1024 * kk:1024 * (kk + 1)], reads=[Q1(kk)], ch="dbg")
    for tt_ in range(8):
        for kk in range(NCH):
            fns = []
            for nb in range(4):
                fns.append(lambda e, kk=kk, nb=nb, tt_=tt_, s=(kk == 0), p=(kk == NCH - 1):
                           e.matmul(out=sb["pa%d" % nb][0:128, 0:512],
                                    lhsT=ar4b[:, 1024 * kk + 128 * tt_: 1024 * kk + 128 * (tt_ + 1)],
                                    rhs=ar1[:, 2048 * kk + 512 * nb: 2048 * kk + 512 * (nb + 1)],
                                    start=s, stop=p))
            K.mmgroup(fns, reads=[Q1(kk), WF(kk)],
                      writes=[("pa%d" % nb, 0, 2048) for nb in range(4)])
        a6 = 4096 * (tt_ % 2)
        S0R = R("AR6", a6, a6 + 2048)
        K.dma("sync", sb["AR6"][:, a6:a6 + 2048], g["S0H"][128 * tt_:128 * (tt_ + 1), :],
              reads=[("S0H", 0, 1)], writes=[S0R], ch="s0ld")
        for nb in range(4):
            evn = "ev0" if nb % 2 == 0 else "ew0"
            ev1n = "ev1" if nb % 2 == 0 else "ew1"
            ev3n = "ev3" if nb % 2 == 0 else "evj"
            E0, E1 = R(evn, 0, 512, 4), R(ev1n, 0, 512, 4)
            E3 = R(ev3n, 0, 512, 4)
            ev3b = sb[ev3n] if ev3n != "evj" else sb["evj"].bitcast(F32)
            K.op("scalar", lambda e, nb=nb, evn=evn: e.activation(out=sb[evn][:, :],
                                                                  in_=sb["pa%d" % nb][0:128, 0:512],
                                                                  func=AF.Silu, scale=sb["invss"][:, 0:1]),
                 reads=[("pa%d" % nb, 0, 2048), ("invss", 0, 4)], writes=[E0])
            K.dma("sync", ev3b[:, 0:512], g["xh"][HALO + 128 * tt_:HALO + 128 * (tt_ + 1),
                                                  512 * nb:512 * (nb + 1)],
                  writes=[E3], ch="xld")
            K.op("vector", lambda e, nb=nb, evn=evn, ev1n=ev1n, a6=a6: e.tensor_tensor(
                out=sb[ev1n][:, :], in0=sb["AR6"][:, a6 + 512 * nb:a6 + 512 * (nb + 1)],
                in1=sb[evn][:, :], op=ALU.mult),
                 reads=[S0R, E0], writes=[E1])
            K.op("vector", lambda e, evn=evn, ev1n=ev1n, ev3b=ev3b: e.tensor_tensor(
                out=sb[evn][:, :], in0=sb[ev1n][:, :], in1=ev3b[:, 0:512], op=ALU.add),
                 reads=[E1, E3], writes=[E0])
            K.dma("sync", g["X1H"][128 * tt_:128 * (tt_ + 1), 512 * nb:512 * (nb + 1)], sb[evn][:, :],
                  reads=[E0], writes=[("X1H", 0, 1)], ch="x1st")

    ar4f = sb["AR4"]
    ln = sb["ln"]
    LNR = R("ln", 0, 16, 4)
    for tt_ in range(8):
        X1R = R("AR4", 0, 2048, 4)
        K.dma("sync", ar4f[:, 0:2048], g["X1H"][128 * tt_:128 * (tt_ + 1), :],
              reads=[("X1H", 0, 1)], writes=[X1R], ch="x1ld")
        K.op("scalar", lambda e: e.activation(out=sb["xsc"][:, :], in_=ar4f[:, 0:2048], func=AF.Copy,
                                              accum_out=ln[:, 0:1]),
             reads=[X1R], writes=[R("xsc", 0, 2048), LNR])
        K.op("scalar", lambda e: e.activation(out=sb["xsc"][:, :], in_=ar4f[:, 0:2048], func=AF.Square,
                                              accum_out=ln[:, 1:2]),
             reads=[X1R], writes=[R("xsc", 0, 2048), LNR])
        K.op("scalar", lambda e: e.mul(out=ln[:, 2:3], in_=ln[:, 0:1], mul=1.0 / 2048.0),
             reads=[LNR], writes=[LNR])
        K.op("vector", lambda e: e.tensor_tensor(out=ln[:, 3:4], in0=ln[:, 2:3], in1=ln[:, 2:3], op=ALU.mult),
             reads=[LNR], writes=[LNR])
        K.op("vector", lambda e: e.tensor_scalar(out=ln[:, 4:5], in0=ln[:, 3:4],
                                                 scalar1=-1.0, scalar2=1e-5, op0=ALU.mult, op1=ALU.add),
             reads=[LNR], writes=[LNR])
        K.op("scalar", lambda e: e.activation(out=ln[:, 5:6], in_=ln[:, 1:2], func=AF.Ln,
                                              scale=1.0 / 2048.0, bias=ln[:, 4:5]),
             reads=[LNR], writes=[LNR])
        K.op("scalar", lambda e: e.activation(out=ln[:, 6:7], in_=ln[:, 5:6], func=AF.Exp, scale=-0.5),
             reads=[LNR], writes=[LNR])
        K.op("vector", lambda e: e.tensor_tensor(out=ln[:, 7:8], in0=ln[:, 2:3], in1=ln[:, 6:7], op=ALU.mult),
             reads=[LNR], writes=[LNR])
        K.op("vector", lambda e: e.tensor_scalar(out=ln[:, 8:9], in0=ln[:, 7:8], scalar1=-1.0,
                                                 scalar2=None, op0=ALU.mult),
             reads=[LNR], writes=[LNR])
        K.op("vector", lambda e: e.tensor_scalar(out=sb["xsc"][:, :], in0=ar4f[:, 0:2048],
                                                 scalar1=ln[:, 6:7], scalar2=ln[:, 8:9],
                                                 op0=ALU.mult, op1=ALU.add),
             reads=[X1R, LNR], writes=[R("xsc", 0, 2048)])
        K.op("vector", lambda e: e.tensor_scalar(out=ln[:, 9:10], in0=ln[:, 1:2], scalar1=1.0 / 2048.0,
                                                 scalar2=None, op0=ALU.mult),
             reads=[LNR], writes=[LNR])
        K.op("vector", lambda e: e.tensor_tensor(out=ln[:, 10:11], in0=ln[:, 9:10], in1=ln[:, 3:4],
                                                 op=ALU.subtract),
             reads=[LNR], writes=[LNR])
        K.op("vector", lambda e: e.tensor_tensor(out=ln[:, 11:12], in0=ln[:, 6:7], in1=ln[:, 6:7],
                                                 op=ALU.mult),
             reads=[LNR], writes=[LNR])
        K.op("vector", lambda e: e.tensor_tensor(out=ln[:, 12:13], in0=ln[:, 10:11], in1=ln[:, 11:12],
                                                 op=ALU.mult),
             reads=[LNR], writes=[LNR])
        K.op("scalar", lambda e: e.activation(out=ln[:, 13:14], in_=ln[:, 12:13], func=AF.Ln,
                                              bias=sb["cst"][:, 1:2]),
             reads=[LNR, R("cst", 1, 2, 4)], writes=[LNR])
        K.op("scalar", lambda e: e.activation(out=ln[:, 14:15], in_=ln[:, 13:14], func=AF.Exp, scale=-0.5),
             reads=[LNR], writes=[LNR])
        K.op("vector", lambda e: e.tensor_reduce(out=sb["cola"][:, 0:1], in_=sb["xsc"][:, :],
                                                 axis=AX.X, op=ALU.max, apply_absolute_value=True),
             reads=[R("xsc", 0, 2048)], writes=[R("cola", 0, 1, 4)])
        K.op("vector", lambda e: e.tensor_scalar(out=sb["cola"][:, 1:2], in0=sb["cola"][:, 0:1],
                                                 scalar1=ln[:, 14:15], scalar2=1.0 / 127.0,
                                                 op0=ALU.mult, op1=ALU.mult),
             reads=[R("cola", 0, 1, 4), LNR], writes=[R("cola", 1, 2, 4)])
        K.op("vector", lambda e, tt_=tt_: e.tensor_copy(out=sb["xscl"][:, tt_:tt_ + 1],
                                                        in_=sb["cola"][:, 1:2]),
             reads=[R("cola", 1, 2, 4)], writes=[R("xscl", tt_, tt_ + 1, 4)])
        K.op("vector", lambda e: e.tensor_scalar(out=sb["cola"][:, 2:3], in0=sb["cola"][:, 1:2],
                                                 scalar1=1e-8, scalar2=None, op0=ALU.add),
             reads=[R("cola", 1, 2, 4)], writes=[R("cola", 2, 3, 4)])
        K.op("vector", lambda e: e.reciprocal(out=sb["cola"][:, 3:4], in_=sb["cola"][:, 2:3]),
             reads=[R("cola", 2, 3, 4)], writes=[R("cola", 3, 4, 4)])
        K.op("vector", lambda e, tt_=tt_: e.tensor_tensor(out=sb["invq"][:, tt_:tt_ + 1],
                                                          in0=sb["cola"][:, 3:4], in1=ln[:, 14:15],
                                                          op=ALU.mult),
             reads=[R("cola", 3, 4, 4), LNR], writes=[R("invq", tt_, tt_ + 1, 4)])
        K.dma("sync", g["XN1TOK"][128 * tt_:128 * (tt_ + 1), :], sb["xsc"][:, :],
              reads=[R("xsc", 0, 2048)], writes=[("XN1TOK", 0, 1)], ch="xtst")
        for kk in range(NCH):
            K.op("tensor", lambda e, kk=kk: e.transpose(out=sb["ptp"][:, 0:128],
                                                        in_=sb["xsc"][:, 128 * kk:128 * (kk + 1)],
                                                        identity=sb["identb_s"][:, :]),
                 reads=[R("xsc", 128 * kk, 128 * (kk + 1)), ("identb_s", 0, 256)],
                 writes=[("ptp", 0, 256)])
            K.op("vector", lambda e, kk=kk: e.tensor_copy(out=sb["wstr"][:, 128 * kk:128 * (kk + 1)],
                                                          in_=sb["ptp"][:, 0:128]),
                 reads=[("ptp", 0, 256)], writes=[R("wstr", 128 * kk, 128 * (kk + 1))])
        K.dma("sync",
              g["XGI"].rearrange("(k p) t -> p k t", p=128)[:, :, 128 * tt_:128 * (tt_ + 1)],
              sb["wstr"][:, 0:2048].rearrange("p (k x) -> p k x", x=128),
              reads=[R("wstr", 0, 2048)], writes=[("XGI", 0, 1)], ch="xgst")
    K.collective("AllGather", [g["XGI"]], [g["XGO"]],
                 reads=[("XGI", 0, 1)], writes=[("XGO", 0, 1)])


def fp8_chunk(nc, sb, K, src, SRCR, dst, DSTR, scname, par=0):
    SC = (scname, 0, 4)
    if par % 2 == 0:
        n0, n1, n2, n3, ni = "ev0", "ev1", "ev2", "ev3", "evi"
    else:
        n0, n1, n2, n3, ni = "ew0", "ew1", "ew2", "ew3", "evj"
    e0, e1, e2, e3 = sb[n0], sb[n1], sb[n2], sb[n3]
    E0, E1, E2, E3 = (R(n0, 0, 512, 4), R(n1, 0, 512, 4),
                      R(n2, 0, 512, 4), R(n3, 0, 512, 4))
    K.op("vector", lambda e: e.tensor_scalar(out=e0[:, :], in0=src, scalar1=sb[scname][:, 0:1],
                                             scalar2=None, op0=ALU.mult),
         reads=[SRCR, SC], writes=[E0])
    K.op("scalar", lambda e: e.activation(out=e1[:, :], in_=e0[:, :], func=AF.Abs),
         reads=[E0], writes=[E1])
    K.op("scalar", lambda e: e.activation(out=e2[:, :], in_=e1[:, :], func=AF.Ln,
                                          bias=sb["cst"][:, 4:5]),
         reads=[E1, R("cst", 4, 5, 4)], writes=[E2])
    K.op("vector", lambda e, ni=ni: e.tensor_scalar(out=sb[ni][:, :], in0=e2[:, :], scalar1=1.0 / LN2,
                                             scalar2=-0.5, op0=ALU.mult, op1=ALU.add),
         reads=[E2], writes=[R(ni, 0, 512, 4)])
    K.op("vector", lambda e, ni=ni: e.tensor_scalar(out=e3[:, :], in0=sb[ni][:, :], scalar1=-6.0,
                                             scalar2=None, op0=ALU.max),
         reads=[R(ni, 0, 512, 4)], writes=[E3])
    K.op("scalar", lambda e: e.activation(out=e2[:, :], in_=e3[:, :], func=AF.Exp,
                                          scale=LN2, bias=sb["cst"][:, 2:3]),
         reads=[E3, R("cst", 2, 3, 4)], writes=[E2])
    K.op("scalar", lambda e: e.activation(out=e1[:, :], in_=e3[:, :], func=AF.Exp,
                                          scale=-LN2, bias=sb["cst"][:, 3:4]),
         reads=[E3, R("cst", 3, 4, 4)], writes=[E1])
    K.op("vector", lambda e: e.tensor_tensor(out=e3[:, :], in0=e0[:, :], in1=e1[:, :], op=ALU.mult),
         reads=[E0, E1], writes=[E3])
    K.op("vector", lambda e, ni=ni: e.tensor_copy(out=sb[ni][:, :], in_=e3[:, :]),
         reads=[E3], writes=[R(ni, 0, 512, 4)])
    K.op("vector", lambda e, ni=ni: e.tensor_copy(out=e0[:, :], in_=sb[ni][:, :]),
         reads=[R(ni, 0, 512, 4)], writes=[E0])
    K.op("vector", lambda e: e.tensor_tensor(out=dst, in0=e0[:, :], in1=e2[:, :], op=ALU.mult),
         reads=[E0, E2], writes=[DSTR])


def st_l1qkv_attn(nc, g, sb, K):
    ar1 = sb["AR1"]
    QH = lambda c0, c1: R("AR1", c0, c1)
    ar4f = sb["AR4"]
    for hp in range(2):
        for wi in range(3):
            K.dma("sync", sb["wstr"][:, 2048 * wi:2048 * (wi + 1)],
                  g["WT1H"][128 * (2 * wi + hp):128 * (2 * wi + hp + 1), :],
                  reads=[("WT1H", 0, 1)], writes=[R("wstr", 2048 * wi, 2048 * (wi + 1))], ch="wt")
        xna = sb["XNA"]
        for tb in range(16):
            rank, half = tb // 2, tb % 2
            for kk in range(NCH):
                slot = (tb * NCH + kk) % 8
                rb = xna[:, 1024 * slot:1024 * slot + 512]
                RB = R("XNA", 1024 * slot, 1024 * slot + 512)
                K.dma("sync", rb[:, 0:512],
                      g["XGO"][2048 * rank + 128 * kk:2048 * rank + 128 * (kk + 1),
                               512 * half:512 * (half + 1)],
                      reads=[("XGO", 0, 1)], writes=[RB], ch="rh%d" % (slot % 4))
                pb_ = ["pa0", "pa1", "pa2"] if tb % 2 == 0 else ["pa3", "pb0", "pa0"]
                fns = []
                for wi in range(3):
                    fns.append(lambda e, wi=wi, kk=kk, rb=rb, pq=pb_[wi], s=(kk == 0), p=(kk == NCH - 1):
                               e.matmul(out=sb[pq][0:128, 0:512],
                                        lhsT=sb["wstr"][:, 2048 * wi + 128 * kk: 2048 * wi + 128 * (kk + 1)],
                                        rhs=rb[:, 0:512], start=s, stop=p))
                K.mmgroup(fns, reads=[RB] + [R("wstr", 2048 * wi, 2048 * (wi + 1)) for wi in range(3)],
                          writes=[(pq, 0, 2048) for pq in pb_])
            pb_ = ["pa0", "pa1", "pa2"] if tb % 2 == 0 else ["pa3", "pb0", "pa0"]
            for wi in range(3):
                dst = ar1[:, 8192 * wi + 512 * tb: 8192 * wi + 512 * (tb + 1)]
                DST = R("AR1", 8192 * wi + 512 * tb, 8192 * wi + 512 * (tb + 1))
                evin = "evi" if tb % 2 == 0 else "evj"
                eng2 = "vector" if wi != 1 else "gpsimd"
                K.op("vector", lambda e, evin=evin, pq=pb_[wi]: e.tensor_scalar(
                    out=sb[evin][:, :], in0=sb[pq][0:128, 0:512],
                    scalar1=100.0, scalar2=None, op0=ALU.mult),
                     reads=[(pb_[wi], 0, 2048)], writes=[R(evin, 0, 512, 4)])
                K.op(eng2, lambda e, dst=dst, evin=evin: e.tensor_scalar(out=dst, in0=sb[evin][:, :],
                                                                  scalar1=0.01, scalar2=None, op0=ALU.mult),
                     reads=[R(evin, 0, 512, 4)], writes=[DST])
        for j in range(64):
            vsl = ar1[:, 16384 + 128 * j:16384 + 128 * (j + 1)]
            VSL = R("AR1", 16384 + 128 * j, 16384 + 128 * (j + 1))
            K.op("tensor", lambda e, vsl=vsl: e.transpose(out=sb["ptp"][:, 0:128], in_=vsl,
                                                          identity=sb["identb_s"][:, :]),
                 reads=[VSL, ("identb_s", 0, 256)], writes=[("ptp", 0, 256)])
            K.op("vector", lambda e, vsl=vsl: e.tensor_copy(out=vsl, in_=sb["ptp"][:, 0:128]),
                 reads=[("ptp", 0, 256)], writes=[VSL])
        K.op("scalar", lambda e: e.activation(out=ar1[:, 24576:32768], in_=ar1[:, 8192:16384],
                                              func=AF.Square),
             reads=[QH(8192, 16384)], writes=[QH(24576, 32768)])
        for blk in range(16):
            K.mmgroup([lambda e, blk=blk: e.matmul(out=sb["pb0"][0:1, 0:512], lhsT=sb["onesc"][:, :],
                                                   rhs=ar1[:, 24576 + 512 * blk:24576 + 512 * (blk + 1)],
                                                   start=True, stop=True)],
                      reads=[QH(24576 + 512 * blk, 24576 + 512 * (blk + 1)), ("onesc", 0, 2)],
                      writes=[("pb0", 0, 2048)])
            K.op("vector", lambda e, blk=blk: e.reduce_max(out=sb["sums"][0:1, blk:blk + 1],
                                                           in_=sb["pb0"][0:1, 0:512], axis=AX.X),
                 reads=[("pb0", 0, 2048)], writes=[R("sums", blk, blk + 1, 4)])
        K.op("vector", lambda e: e.reduce_max(out=sb["sums"][0:1, 16:17], in_=sb["sums"][0:1, 0:16],
                                              axis=AX.X),
             reads=[R("sums", 0, 16, 4)], writes=[R("sums", 16, 17, 4)])
        K.op("scalar", lambda e: e.activation(out=sb["sums"][0:1, 17:18], in_=sb["sums"][0:1, 16:17],
                                              func=AF.Sqrt, scale=1.0 / 128.0),
             reads=[R("sums", 16, 17, 4)], writes=[R("sums", 17, 18, 4)])
        K.op("scalar", lambda e: e.mul(out=sb["sums"][0:1, 19:20], in_=sb["sums"][0:1, 17:18], mul=-1.0),
             reads=[R("sums", 17, 18, 4)], writes=[R("sums", 19, 20, 4)])
        repl_scalar(K, sb, sb["sums"][0:1, 19:20], R("sums", 19, 20, 4),
                    sb["colb"][:, 2:3], R("colb", 2, 3, 4))
        K.op("scalar", lambda e: e.activation(out=sb["AR6"][:, 0:8192], in_=ar1[:, 0:8192],
                                              func=AF.Square),
             reads=[QH(0, 8192)], writes=[R("AR6", 0, 8192)])
        for half in range(2):
            fns = []
            for i2 in range(32):
                i = 32 * half + i2
                fns.append(lambda e, i=i, i2=i2, half=half: e.matmul(
                    out=sb["pb0"][0:128, 32 * (half % 1) + i2:32 * (half % 1) + i2 + 1] if False else sb["pb0"][0:128, i2:i2 + 1],
                    lhsT=sb["AR6"][:, 128 * i:128 * (i + 1)],
                    rhs=sb["onesc"][:, :], start=True, stop=True))
            K.mmgroup(fns, reads=[R("AR6", 4096 * half, 4096 * (half + 1)), ("onesc", 0, 2)],
                      writes=[("pb0", 0, 2048)])
            K.op("scalar", lambda e, half=half: e.activation(out=sb["cola"][:, 4:5] if False else sb["nbq"][:, 32 * half:32 * (half + 1)],
                                                             in_=sb["pb0"][0:128, 0:32], func=AF.Sqrt),
                 reads=[("pb0", 0, 2048)], writes=[R("nbq", 32 * half, 32 * (half + 1), 4)])
        K.op("vector", lambda e: e.tensor_scalar(out=sb["nbq"][:, 0:64], in0=sb["nbq"][:, 0:64],
                                                 scalar1=sb["colb"][:, 2:3], scalar2=None, op0=ALU.mult),
             reads=[R("nbq", 0, 64, 4), R("colb", 2, 3, 4)], writes=[R("nbq", 0, 64, 4)])
        for i in range(NTQ):
            nt = i + 1
            nblk = (nt * 128 + 511) // 512
            psm_c = 128 * (i % 2)
            ab = 4096 * (i % 2) if nt <= 32 else 0
            abf = ab * 4
            for blk in range(nblk):
                w = min(512, 128 * nt - 512 * blk)
                psc = "pa3" if blk % 2 == 0 else "pa2"
                K.mmgroup([lambda e, i=i, blk=blk, w=w, psc=psc: e.matmul(
                    out=sb[psc][0:128, 0:w], lhsT=ar1[:, 128 * i:128 * (i + 1)],
                    rhs=ar1[:, 8192 + 512 * blk:8192 + 512 * blk + w], start=True, stop=True)],
                    reads=[QH(0, 8192), QH(8192, 16384)], writes=[(psc, 0, 2048)])
                if blk == nblk - 1:
                    K.op("vector", lambda e, psc=psc, w=w: e.tensor_tensor(
                        out=sb[psc][0:128, w - 128:w], in0=sb[psc][0:128, w - 128:w],
                        in1=sb["trilng"][:, :], op=ALU.add),
                        reads=[(psc, 0, 2048), ("trilng", 0, 512)], writes=[(psc, 0, 2048)])
                sco = 20 * (i % 2)
                K.op("scalar", lambda e, blk=blk, w=w, psc=psc, i=i, ab=ab, sco=sco: e.activation(
                    out=ar4f[:, ab + 512 * blk:ab + 512 * blk + w], in_=sb[psc][0:128, 0:w],
                    func=AF.Exp, scale=ISQHD, bias=sb["nbq"][:, i:i + 1],
                    accum_out=sb["sums"][:, sco + blk:sco + blk + 1]),
                    reads=[(psc, 0, 2048), R("nbq", i, i + 1, 4)],
                    writes=[R("AR4", ab + 512 * blk, ab + 512 * blk + w, 4),
                            R("sums", sco + blk, sco + blk + 1, 4)])
            sco = 20 * (i % 2)
            K.op("vector", lambda e, nblk=nblk, sco=sco: e.tensor_reduce(
                out=sb["sums"][:, sco + 16:sco + 17], in_=sb["sums"][:, sco:sco + nblk],
                axis=AX.X, op=ALU.add),
                reads=[R("sums", sco, sco + 16, 4)], writes=[R("sums", sco + 16, sco + 17, 4)])
            K.op("vector", lambda e, sco=sco: e.reciprocal(out=sb["sums"][:, sco + 17:sco + 18],
                                                           in_=sb["sums"][:, sco + 16:sco + 17]),
                 reads=[R("sums", sco + 16, sco + 17, 4)], writes=[R("sums", sco + 17, sco + 18, 4)])
            wrow = 128 * nt
            ar4i = ar4f.bitcast(I32)
            wv = min(wrow, ((wrow * 4 // 5) + 511) // 512 * 512)
            parts = [("vector", 0, wv)] + ([("gpsimd", wv, wrow)] if wv < wrow else [])
            for engn, p0, p1 in parts:
                PR = R("AR4", ab + p0, ab + p1, 4)
                K.op(engn, lambda e, p0=p0, p1=p1, ab=ab, sco=sco: e.tensor_scalar(
                    out=ar4i[:, ab + p0:ab + p1], in0=ar4f[:, ab + p0:ab + p1],
                    scalar1=sb["sums"][:, sco + 17:sco + 18], scalar2=100.0,
                    op0=ALU.mult, op1=ALU.mult),
                    reads=[PR, R("sums", sco + 17, sco + 18, 4)], writes=[PR])
                K.op(engn, lambda e, p0=p0, p1=p1, ab=ab: e.tensor_copy(
                    out=sb["AR6"][:, ab + p0:ab + p1], in_=ar4i[:, ab + p0:ab + p1]),
                    reads=[PR], writes=[R("AR6", ab + p0, ab + p1)])
            GS = 8
            ngrp = (nt + GS - 1) // GS
            def tctx(g_):
                return (("ptp", 0) if g_ % 2 == 0 else ("ptq", 0))
            def bctx(g_):
                if g_ % 2 == 0:
                    return ("ptb", 0, "ptb")
                return ("wstr", 4096, "wstr")
            def emit_T(g_):
                j0, j1 = GS * g_, min(GS * g_ + GS, nt)
                pn, h0 = tctx(g_)
                for j in range(j0, j1):
                    o = h0 + 128 * (j - j0)
                    K.op("tensor", lambda e, j=j, o=o, ab=ab, pn=pn: e.transpose(
                        out=sb[pn][:, o:o + 128], in_=sb["AR6"][:, ab + 128 * j:ab + 128 * (j + 1)],
                        identity=sb["identb_s"][:, :]),
                        reads=[R("AR6", ab + 128 * j, ab + 128 * (j + 1)), ("identb_s", 0, 256)],
                        writes=[(pn, 0, 1)])
            def emit_PV(g_):
                j0, j1 = GS * g_, min(GS * g_ + GS, nt)
                pn, h0 = tctx(g_)
                bn, b0, breg = bctx(g_)
                K.op("vector", lambda e, h0=h0, b0=b0, n=(j1 - j0), pn=pn, bn=bn: e.tensor_copy(
                    out=sb[bn][:, b0:b0 + 128 * n], in_=sb[pn][:, h0:h0 + 128 * n]),
                    reads=[(pn, 0, 1)],
                    writes=[R(breg, b0, b0 + 1024)])
                fns = []
                for j in range(j0, j1):
                    o = b0 + 128 * (j - j0)
                    fns.append(lambda e, j=j, o=o, bn=bn, s=(j == 0), p=(j == nt - 1): e.matmul(
                        out=sb["psm"][0:128, psm_c:psm_c + 128],
                        lhsT=ar1[:, 16384 + 128 * j:16384 + 128 * (j + 1)],
                        rhs=sb[bn][:, o:o + 128], start=s, stop=p))
                K.mmgroup(fns, reads=[QH(16384 + 128 * j0, 16384 + 128 * j1),
                                      R(breg, b0, b0 + 512)],
                          writes=[("psm", 2 * psm_c, 2 * psm_c + 512)])
            emit_T(0)
            if ngrp > 1:
                emit_T(1)
            for g_ in range(2, ngrp):
                emit_PV(g_ - 2)
                emit_T(g_)
            if ngrp > 1:
                emit_PV(ngrp - 2)
            emit_PV(ngrp - 1)
            K.op("vector", lambda e, i=i: e.tensor_copy(out=ar1[:, 24576 + 128 * i:24576 + 128 * (i + 1)],
                                                        in_=sb["psm"][0:128, psm_c:psm_c + 128]),
                 reads=[("psm", 2 * psm_c, 2 * psm_c + 512)],
                 writes=[QH(24576 + 128 * i, 24576 + 128 * (i + 1))])
        K.dma("sync", g["AOI"][128 * hp:128 * (hp + 1), :], ar1[:, 24576:32768],
              reads=[QH(24576, 32768)], writes=[("AOI", 0, 1)], ch="aost")
    K.collective("AllGather", [g["AOI"]], [g["AOO"]],
                 reads=[("AOI", 0, 1)], writes=[("AOO", 0, 1)])


def st_l1out(nc, g, sb, K):
    ar1 = sb["AR1"]
    ar4b = sb["AR4"].bitcast(BF16)
    WT = lambda kk: R("AR1", 2048 * kk, 2048 * (kk + 1))
    AO = lambda kk: R("AR4", 1024 * kk, 1024 * (kk + 1), 2)

    def load_wo(kk, widx):
        for rb in range(NCORE):
            r0 = rb * WSL + widx * 256
            K.dma("sync", ar1[:, 2048 * kk + 256 * rb: 2048 * kk + 256 * (rb + 1)],
                  g["WAGO"][r0:r0 + 256, 128 * kk:128 * (kk + 1)],
                  reads=[("WAGO", 0, 1)], writes=[WT(kk)], transpose=True, ch="wl")

    def load_ao(kk):
        K.dma("sync", ar4b[:, 1024 * kk:1024 * (kk + 1)],
              lambda e, kk=kk: g["AOO"][128 * kk:128 * (kk + 1), bass.ts(K.pid(e), T)],
              reads=[("AOO", 0, 1)], writes=[AO(kk)], ch="ao")

    for tt_ in range(8):
        for kk in range(NCH):
            if tt_ == 0:
                load_ao(kk)
                load_wo(kk, W_O1)
            fns = []
            for nb in range(4):
                fns.append(lambda e, kk=kk, nb=nb, tt_=tt_, s=(kk == 0), p=(kk == NCH - 1):
                           e.matmul(out=sb["pa%d" % nb][0:128, 0:512],
                                    lhsT=ar4b[:, 1024 * kk + 128 * tt_: 1024 * kk + 128 * (tt_ + 1)],
                                    rhs=ar1[:, 2048 * kk + 512 * nb: 2048 * kk + 512 * (nb + 1)],
                                    start=s, stop=p))
            K.mmgroup(fns, reads=[AO(kk), WT(kk)],
                      writes=[("pa%d" % nb, 0, 2048) for nb in range(4)])
        for nb in range(4):
            E0, E1, E2 = R("ev0", 0, 512, 4), R("ev1", 0, 512, 4), R("ev2", 0, 512, 4)
            evin = "evi" if nb % 2 == 0 else "evj"
            ev2n = "ev2" if nb % 2 == 0 else "ew2"
            E2 = R(ev2n, 0, 512, 4)
            K.op("vector", lambda e, nb=nb, evin=evin: e.tensor_copy(out=sb[evin][:, :],
                                                                     in_=sb["pa%d" % nb][0:128, 0:512]),
                 reads=[("pa%d" % nb, 0, 2048)], writes=[R(evin, 0, 512, 4)])
            K.op("vector", lambda e, evin=evin, ev2n=ev2n: e.tensor_copy(out=sb[ev2n][:, :],
                                                                         in_=sb[evin][:, :]),
                 reads=[R(evin, 0, 512, 4)], writes=[E2])
            K.dma("sync", g["S1H"][128 * tt_:128 * (tt_ + 1), 512 * nb:512 * (nb + 1)],
                  sb[ev2n][:, :], reads=[E2], writes=[("S1H", 0, 1)], ch="s1st")

    ar4b2 = sb["AR4"].bitcast(BF16)
    for tt_ in range(8):
        if tt_ % 2 == 0:
            xnt, XT = sb["xsc"][:, 0:2048], R("xsc", 0, 2048)
        else:
            xnt, XT = sb["XNA"][:, 0:2048], R("XNA", 0, 2048)
        K.dma("sync", xnt, g["XN1TOK"][128 * tt_:128 * (tt_ + 1), :],
              reads=[("XN1TOK", 0, 1)], writes=[XT], ch="xtld%d" % (tt_ % 2))
        for cb in range(4):
            E0, EI = R("ev0", 0, 512, 4), R("evi", 0, 512, 4)
            evin = "evi" if cb % 2 == 0 else "evj"
            EI = R(evin, 0, 512, 4)
            K.op("vector", lambda e, cb=cb, tt_=tt_, evin=evin, xnt=xnt: e.tensor_scalar(
                out=sb[evin][:, :], in0=xnt[:, 512 * cb:512 * (cb + 1)],
                scalar1=sb["invq"][:, tt_:tt_ + 1], scalar2=None, op0=ALU.mult),
                reads=[XT, R("invq", tt_, tt_ + 1, 4)], writes=[EI])
            w0_ = 2048 * (tt_ % 2)
            K.op("vector", lambda e, cb=cb, tt_=tt_, evin=evin, w0_=w0_: e.tensor_scalar(
                out=sb["wstr"][:, w0_ + 512 * cb:w0_ + 512 * (cb + 1)], in0=sb[evin][:, :],
                scalar1=sb["xscl"][:, tt_:tt_ + 1], scalar2=None, op0=ALU.mult),
                reads=[EI, R("xscl", tt_, tt_ + 1, 4)],
                writes=[R("wstr", w0_ + 512 * cb, w0_ + 512 * (cb + 1))])
        w0_ = 2048 * (tt_ % 2)
        ptn = "ptp" if tt_ % 2 == 0 else "ptq"
        for kk in range(NCH):
            K.op("tensor", lambda e, kk=kk, w0_=w0_, ptn=ptn: e.transpose(
                out=sb[ptn][:, 0:128], in_=sb["wstr"][:, w0_ + 128 * kk:w0_ + 128 * (kk + 1)],
                identity=sb["identb_s"][:, :]),
                 reads=[R("wstr", w0_ + 128 * kk, w0_ + 128 * (kk + 1)), ("identb_s", 0, 256)],
                 writes=[(ptn, 0, 1)])
            K.op("vector", lambda e, kk=kk, w0_=w0_, ptn=ptn: e.tensor_copy(
                out=sb["AR6"][:, w0_ + 128 * kk:w0_ + 128 * (kk + 1)], in_=sb[ptn][:, 0:128]),
                 reads=[(ptn, 0, 1)], writes=[R("AR6", w0_ + 128 * kk, w0_ + 128 * (kk + 1))])
        for kk in range(NCH):
            if tt_ == 0:
                load_wo(kk, W_F1)
            fns = []
            for nb in range(4):
                fns.append(lambda e, kk=kk, nb=nb, w0_=w0_, s=(kk == 0):
                           e.matmul(out=sb["pa%d" % nb][0:128, 0:512],
                                    lhsT=sb["AR6"][:, w0_ + 128 * kk:w0_ + 128 * (kk + 1)],
                                    rhs=ar1[:, 2048 * kk + 512 * nb: 2048 * kk + 512 * (nb + 1)],
                                    start=s, stop=False))
            K.mmgroup(fns, reads=[R("AR6", w0_ + 128 * kk, w0_ + 128 * (kk + 1)), WT(kk)],
                      writes=[("pa%d" % nb, 0, 2048) for nb in range(4)])
        fns = []
        for nb in range(4):
            fns.append(lambda e, nb=nb: e.matmul(out=sb["pa%d" % nb][0:128, 0:512],
                                                 lhsT=sb["onesr"][0:1, :],
                                                 rhs=sb["bpos"][0:1, 512 * nb:512 * (nb + 1)],
                                                 start=False, stop=True))
        K.mmgroup(fns, reads=[("onesr", 0, 2), ("bpos", 0, 4096)],
                  writes=[("pa%d" % nb, 0, 2048) for nb in range(4)])
        a0 = 4096 * (tt_ % 2)
        S1R = R("AR4", a0, a0 + 2048, 4)
        X1R2 = R("AR4", a0 + 2048, a0 + 4096, 4)
        ar4f3 = sb["AR4"]
        K.dma("sync", ar4f3[:, a0:a0 + 2048], g["S1H"][128 * tt_:128 * (tt_ + 1), :],
              reads=[("S1H", 0, 1)], writes=[S1R], ch="s1ld")
        K.dma("sync", ar4f3[:, a0 + 2048:a0 + 4096], g["X1H"][128 * tt_:128 * (tt_ + 1), :],
              reads=[("X1H", 0, 1)], writes=[X1R2], ch="x1ld")
        for nb in range(4):
            evn = "ev0" if nb % 2 == 0 else "ew0"
            ev2n = "ev2" if nb % 2 == 0 else "ew2"
            E0, E2 = R(evn, 0, 512, 4), R(ev2n, 0, 512, 4)
            K.op("scalar", lambda e, nb=nb, evn=evn: e.activation(out=sb[evn][:, :],
                                                                  in_=sb["pa%d" % nb][0:128, 0:512],
                                                                  func=AF.Silu),
                 reads=[("pa%d" % nb, 0, 2048)], writes=[E0])
            K.op("vector", lambda e, nb=nb, evn=evn, ev2n=ev2n, a0=a0: e.tensor_tensor(
                out=sb[ev2n][:, :], in0=sb[evn][:, :],
                in1=ar4f3[:, a0 + 512 * nb:a0 + 512 * (nb + 1)], op=ALU.mult),
                 reads=[E0, S1R], writes=[E2])
            K.op("vector", lambda e, nb=nb, evn=evn, ev2n=ev2n, a0=a0: e.scalar_tensor_tensor(
                out=sb[evn][:, :], in0=sb[ev2n][:, :], scalar=0.01,
                in1=ar4f3[:, a0 + 2048 + 512 * nb:a0 + 2048 + 512 * (nb + 1)],
                op0=ALU.mult, op1=ALU.add),
                 reads=[E2, X1R2], writes=[E0])
            K.dma("sync", g["out"][128 * tt_:128 * (tt_ + 1), 512 * nb:512 * (nb + 1)],
                  sb[evn][:, :], reads=[E0], writes=[("out", 0, 1)], ch="ost")
    K.fence_all("sync")



def make_inputs(inp):
    import ml_dtypes
    x = np.asarray(inp["L0_attn_hidden_states"], np.float32).reshape(S, D)
    zeros = np.zeros((HALO, D), np.float32)

    inv = (1.0 / 10000.0 ** (np.arange(0, HD, 2, dtype=np.float64) / HD)).astype(np.float32)
    f = (np.arange(S, dtype=np.float32)[:, None] * inv[None, :]).astype(np.float32)
    cosT = np.cos(f.astype(np.float64)).astype(np.float32)
    sinT = np.sin(f.astype(np.float64)).astype(np.float32)
    cos128 = np.ascontiguousarray(np.concatenate([cosT, cosT], 1).T)
    sin128 = np.ascontiguousarray(np.concatenate([sinT, sinT], 1).T)
    zt = np.zeros((128, HALO), np.float32)

    NEG = np.float32(-1e30)
    trilA = np.where(np.tril(np.ones((WIN, WIN), bool)), np.float32(0), NEG)
    mask_first = np.concatenate([np.full((WIN, WIN), NEG), trilA], 1).astype(np.float32)
    mask_std = np.concatenate([np.zeros((WIN, WIN), np.float32), trilA], 1)
    tril1 = np.tril(np.ones((128, 128), np.float32)).astype(ml_dtypes.bfloat16)
    trilneg = np.where(np.tril(np.ones((128, 128), bool)), np.float32(0), np.float32(-1e30))
    identb = np.eye(128).astype(ml_dtypes.bfloat16)
    rotm = np.zeros((128, 128), np.float64)
    for j in range(64):
        rotm[j + 64, j] = -1.0
        rotm[j, j + 64] = 1.0
    rotmb = rotm.astype(ml_dtypes.bfloat16)
    ones_c = np.ones((128, 1), ml_dtypes.bfloat16)
    ones_r = np.ones((1, 128), ml_dtypes.bfloat16)

    in_maps = []
    for c in range(NCORE):
        t0 = c * T
        xhs = np.concatenate([zeros if c == 0 else x[t0 - HALO:t0], x[t0:t0 + T]], 0)
        rc = np.concatenate([zt if c == 0 else cos128[:, t0 - HALO:t0], cos128[:, t0:t0 + T]], 1)
        rs = np.concatenate([zt if c == 0 else sin128[:, t0 - HALO:t0], sin128[:, t0:t0 + T]], 1)
        m0 = np.concatenate([mask_first if c == 0 else mask_std, mask_std], 1)
        m = {
            "xh": np.ascontiguousarray(xhs),
            "wq0s": np.asarray(inp["L0_attn_Wq"], np.float32)[256 * c:256 * (c + 1)],
            "wk0s": np.asarray(inp["L0_attn_Wk"], np.float32)[256 * c:256 * (c + 1)],
            "wv0s": np.asarray(inp["L0_attn_Wv"], np.float32)[256 * c:256 * (c + 1)],
            "wo0s": np.asarray(inp["L0_attn_Wo"], np.float32)[256 * c:256 * (c + 1)],
            "w2s": np.asarray(inp["L0_ffn_input2"], np.float32)[256 * c:256 * (c + 1)],
            "wo1s": np.asarray(inp["L1_attn_Wo"], np.float32)[256 * c:256 * (c + 1)],
            "wf1s": np.asarray(inp["L1_ffn_weight"], np.float32)[256 * c:256 * (c + 1)],
            "wq1s": np.asarray(inp["L1_attn_Wq"], np.float32)[256 * c:256 * (c + 1)],
            "wk1s": np.asarray(inp["L1_attn_Wk"], np.float32)[256 * c:256 * (c + 1)],
            "wv1s": np.asarray(inp["L1_attn_Wv"], np.float32)[256 * c:256 * (c + 1)],
            "b1": np.asarray(inp["L1_ffn_bias"], np.float32).reshape(1, D),
            "sc0": np.full((128, 1), np.asarray(inp["L0_ffn_scale"], np.float32), np.float32),
            "sc1": np.full((128, 1), np.asarray(inp["L0_ffn_scale1"], np.float32), np.float32),
            "rcosT": np.ascontiguousarray(rc),
            "rsinT": np.ascontiguousarray(rs),
            "mask0": m0.astype(ml_dtypes.bfloat16),
            "tril1": tril1,
            "trilneg": trilneg,
            "identb": identb,
            "rotmb": rotmb,
            "ones_c": ones_c,
            "ones_r": ones_r,
        }
        in_maps.append(m)
    return in_maps


def kernel(**inputs):
    nc = build_nc()
    in_maps = make_inputs(inputs)
    res = run_bass_kernel_spmd(nc, in_maps, core_ids=list(range(NCORE)))
    out = np.concatenate([res.results[i]["out"] for i in range(NCORE)], 0)
    return np.ascontiguousarray(out.reshape(1, S, D), dtype=np.float32)
